# version 03 (final)
# baseline (speedup 1.0000x reference)
import numpy as np
import concourse.bass as bass
import concourse.mybir as mybir
from concourse.bass_utils import run_bass_kernel_spmd

F32 = mybir.dt.float32
BF16 = mybir.dt.bfloat16
AX = mybir.AxisListType
OP = mybir.AluOpType
ACT = mybir.ActivationFunctionType
APc = bass.AP

T = 4096
D = 1024
NCH = 8
NEG = -1e30


ENGINES = ["tensor", "vector", "scalar", "gpsimd", "sync"]


SERIAL = [False]


NLANE = 4


class Sched:
    def __init__(self, nc):
        self.nc = nc
        self.sem = {}
        self.cnt = {e: 0 for e in ENGINES}
        self.dma_cnt = {e: 0 for e in ENGINES}
        self.lane_cnt = {}
        for e in ENGINES:
            self.sem[e] = nc.alloc_semaphore(f"sem_{e}")
            for ln in range(NLANE):
                self.sem[f"{e}_dma{ln}"] = nc.alloc_semaphore(f"sem_{e}_dma{ln}")
                self.lane_cnt[f"{e}_dma{ln}"] = 0
        self.hist = {}
        self.floor = {}
        self.steps = []

    def _w(self, waits, ent):
        k, val = ent
        waits[k] = max(waits.get(k, 0), val)

    def step(self, engine, emit, reads=(), writes=(), is_dma=False):
        waits = {}
        if SERIAL[0]:
            for e in ENGINES:
                if e != engine and self.cnt[e]:
                    waits[e] = self.cnt[e]
                for ln in range(NLANE):
                    k = f"{e}_dma{ln}"
                    if self.lane_cnt[k] and not (is_dma and e == engine):
                        waits[k] = self.lane_cnt[k]
        for t in reads:
            h = self.hist.get(t)
            if h and h[0]:
                self._w(waits, h[0])
        for t in writes:
            h = self.hist.get(t)
            if h:
                if h[0]:
                    self._w(waits, h[0])
                for r in h[1]:
                    self._w(waits, r)
        fw = {}
        for k, v in waits.items():
            if is_dma and k.startswith(engine + "_dma"):
                continue
            if k == engine:
                if engine == "tensor":
                    continue
            if self.floor.get((engine, k), 0) >= v:
                continue
            fw[k] = v
        if is_dma:
            lane = f"{engine}_dma{self.dma_cnt[engine] % NLANE}"
            self.dma_cnt[engine] += 1
            self.lane_cnt[lane] += 16
            ent = (lane, self.lane_cnt[lane])
            inc = (self.sem[lane], self.lane_cnt[lane], 16)
        else:
            self.cnt[engine] += 1
            ent = (engine, self.cnt[engine])
            inc = (self.sem[engine], self.cnt[engine], 1)
        self.steps.append((engine, fw, emit, inc))
        for k, v in fw.items():
            self.floor[(engine, k)] = v
        for t in reads:
            self.hist.setdefault(t, [None, []])[1].append(ent)
        for t in writes:
            self.hist[t] = [ent, []]
        return ent

    def build(self, blk):
        per = {e: [] for e in ENGINES}
        for (engine, fw, emit, inc) in self.steps:
            per[engine].append((fw, emit, inc))
        for e in ENGINES:
            lst = per[e]
            def make(fn_list):
                def body(eng):
                    for (fw, emit, inc) in fn_list:
                        for k, v in fw.items():
                            eng.wait_ge(self.sem[k], v)
                        ins = emit(eng)
                        ins.then_inc(inc[0], inc[2])
                return body
            getattr(blk, e)(make(lst))

    @staticmethod
    def keys(*aps):
        out = []
        for a in aps:
            if isinstance(a, bass.AP):
                n = a.tensor.name
                out.append(KEY_ALIAS.get(n, n))
        return out

    def dma(self, engine, out, in_, **kw):
        return self.step(engine, lambda e: e.dma_start(out, in_, **kw),
                         reads=self.keys(in_), writes=self.keys(out), is_dma=True)

    def dmaT(self, engine, out, in_):
        return self.step(engine, lambda e: e.dma_start_transpose(out, in_),
                         reads=self.keys(in_), writes=self.keys(out), is_dma=True)

    def mm(self, out, lhsT, rhs, start=True, stop=True, tp=None):
        return self.step("tensor",
                         lambda e: e.matmul(out, lhsT, rhs, start=start, stop=stop,
                                            tile_position=tp),
                         reads=self.keys(lhsT, rhs), writes=self.keys(out))

    def mmgroup(self, out, parts, start=True, stop=True, tp=None):
        rd = []
        for (l, r) in parts:
            rd += self.keys(l, r)
        n = len(parts)
        def emit(e):
            ins = None
            for i, (l, r) in enumerate(parts):
                ins = e.matmul(out, l, r, start=(start and i == 0),
                               stop=(stop and i == n - 1), tile_position=tp)
            return ins
        return self.step("tensor", emit, reads=rd, writes=self.keys(out))

    def trans(self, out, in_, ident):
        return self.step("tensor", lambda e: e.transpose(out, in_, ident),
                         reads=self.keys(in_, ident), writes=self.keys(out))

    def act(self, func, out, in_, bias=0.0, scale=1.0, accum_out=None):
        wr = self.keys(out, accum_out) if accum_out is not None else self.keys(out)
        return self.step("scalar",
                         lambda e: e.activation(out, in_, func, bias=bias, scale=scale,
                                                accum_out=accum_out),
                         reads=self.keys(in_, bias, scale), writes=wr)

    def tt(self, op, out, a, b, engine="vector"):
        return self.step(engine, lambda e: e.tensor_tensor(out, a, b, op),
                         reads=self.keys(a, b), writes=self.keys(out))

    def ts(self, out, a, s1, s2, op0, op1=None, engine="vector"):
        if s2 is None and op1 is None:
            return self.step(engine, lambda e: e.tensor_single_scalar(out, a, s1, op0),
                             reads=self.keys(a, s1), writes=self.keys(out))
        return self.step(engine, lambda e: e.tensor_scalar(out, a, s1, s2, op0, op1),
                         reads=self.keys(a, s1, s2), writes=self.keys(out))

    def stt(self, out, in0, scalar, in1, op0, op1, engine="vector"):
        return self.step(engine,
                         lambda e: e.scalar_tensor_tensor(out, in0, scalar, in1, op0, op1),
                         reads=self.keys(in0, scalar, in1), writes=self.keys(out))

    def copy(self, out, in_, engine="vector"):
        return self.step(engine, lambda e: e.tensor_copy(out, in_),
                         reads=self.keys(in_), writes=self.keys(out))

    def red(self, op, out, in_, absval=None, engine="vector"):
        return self.step(engine,
                         lambda e: e.tensor_reduce(out, in_, AX.X, op,
                                                   apply_absolute_value=absval),
                         reads=self.keys(in_), writes=self.keys(out))

    def recip(self, out, in_):
        return self.step("vector", lambda e: e.reciprocal(out, in_),
                         reads=self.keys(in_), writes=self.keys(out))

    def memset(self, out, val, engine="vector"):
        return self.step(engine, lambda e: e.memset(out, val), writes=self.keys(out))

    def nopwait(self, engine, reads):
        return self.step(engine, lambda e: e.nop(), reads=reads)

    def nopwait(self, engine, reads):
        return self.step(engine, lambda e: e.nop(), reads=reads)


KEY_ALIAS = {}


class PsumPool:
    def __init__(self, nc, n=8):
        self.h = [nc.alloc_psum_tensor(f"pb{i}", [128, 512], F32) for i in range(n)]
        self.b = [h.ap() for h in self.h]
        self.bb = []
        for i, h in enumerate(self.h):
            hb = h.bitcast(BF16)
            KEY_ALIAS[hb.name] = h.name
            self.bb.append(hb.ap())
        self.i = 0

    def get(self):
        t = self.b[self.i % 6]
        self.i += 1
        return t

    def get_bf(self):
        t = self.bb[self.i % 6]
        self.i += 1
        return t

    def resv(self, i):
        return self.b[6 + i]


def make_consts():
    c = {}
    c["c_ones1"] = np.ones((1, 128), np.float32)
    c["c_onescol"] = np.ones((128, 1), np.float32)
    tri64 = np.triu(np.ones((64, 64), np.float32))
    c["c_triu64"] = tri64
    c["c_ntriu64"] = -tri64
    c["c_stril64"] = (np.arange(64)[:, None] > np.arange(64)[None, :]).astype(np.float32)
    c["c_eye8"] = np.eye(8, dtype=np.float32)
    bd = np.zeros((128, 128), np.float32)
    bd[:64, :64] = tri64; bd[64:, 64:] = tri64
    c["c_triu_bd"] = bd
    sl = (np.arange(64)[:, None] > np.arange(64)[None, :]).astype(np.float32)
    bd2 = np.zeros((128, 128), np.float32)
    bd2[:64, :64] = sl; bd2[64:, 64:] = sl
    c["c_stril_bd"] = bd2
    m = np.where(np.arange(64)[:, None] > np.arange(64)[None, :], 0.0, NEG)
    c["c_maskD"] = np.ascontiguousarray(np.concatenate([m] * 8, axis=1))
    mT = np.where(np.arange(64)[:, None] < np.arange(64)[None, :], 0.0, NEG)
    c["c_maskDT"] = np.ascontiguousarray(np.concatenate([mT] * 8, axis=1))
    c["c_eye128"] = np.eye(128, dtype=np.float32)

    DH, WIN = 64, 512
    half_idx = np.arange(0, DH, 2, dtype=np.float32)
    inv_freq = 1.0 / 10000.0 ** (half_idx / DH)
    t = np.arange(T, dtype=np.float32)
    freqs = np.concatenate([t[:, None] * inv_freq[None, :]] * 2, -1)
    base = (half_idx + 0.4 * DH) / (1.4 * DH)
    power = (t - T // 2) / float(WIN // 2)
    scale = np.concatenate([base[None, :] ** power[:, None]] * 2, -1)
    cos, sin = np.cos(freqs), np.sin(freqs)
    rep = lambda a: np.ascontiguousarray(np.tile(np.asarray(a, np.float32).T, (2, 1)))
    c["c_xpos_cq"] = rep(cos * scale)
    c["c_xpos_sq"] = rep(sin * scale)
    c["c_xpos_ck"] = rep(cos / scale)
    c["c_xpos_sk"] = rep(sin / scale)
    P = np.zeros((64, 64), np.float32)
    for i in range(32):
        P[i, i + 32] = -1.0
        P[i + 32, i] = 1.0
    P128 = np.zeros((128, 128), np.float32)
    P128[:64, :64] = P; P128[64:, 64:] = P
    c["c_perm_xpos"] = np.ascontiguousarray(P128.T)

    for b in range(8):
        jc = b * 128 + np.arange(128)[:, None]
        i = np.arange(512)[None, :]
        keep = (i <= jc) & (jc <= i + 512)
        c[f"c_la_mB{b}"] = keep.astype(np.float32)

    half = 32
    i = np.arange(half, dtype=np.float32)
    theta = 10000.0 ** (-2.0 * (i - 1.0) / 64.0)
    ang = t[:, None] * theta[None, :]
    rc = np.zeros((T, 64), np.float32); rs = np.zeros((T, 64), np.float32)
    rc[:, 0::2] = np.cos(ang); rc[:, 1::2] = np.cos(ang)
    rs[:, 0::2] = np.sin(ang); rs[:, 1::2] = -np.sin(ang)
    c["c_rope_c"] = rep(rc)
    c["c_rope_s"] = rep(rs)
    Q = np.zeros((64, 64), np.float32)
    for k in range(32):
        Q[2 * k, 2 * k + 1] = 1.0
        Q[2 * k + 1, 2 * k] = 1.0
    Q128 = np.zeros((128, 128), np.float32)
    Q128[:64, :64] = Q; Q128[64:, 64:] = Q
    c["c_perm_rope"] = np.ascontiguousarray(Q128.T)

    for b in range(4):
        r = np.arange(128)[:, None]
        cc = np.arange(512)[None, :]
        c[f"c_l1diag{b}"] = (cc >= 128 * b + r).astype(np.float32)
    hs = np.zeros((2, 128), np.float32)
    hs[0, :64] = 1.0; hs[1, 64:] = 1.0
    c["c_halfsel"] = hs
    hsT = np.ascontiguousarray(hs.T)
    c["c_halfselT"] = hsT
    return c


CONSTS = make_consts()


def _dtsize(dt):
    return 2 if dt == BF16 else 4


class Arena:
    def __init__(self, nc, base, top):
        self.nc, self.base, self.top, self.off = nc, base, top, 0
        self.uid = 0

    def alloc(self, name, shape, dt=F32):
        off = (self.off + 63) & ~63
        self.uid += 1
        t = self.nc.alloc_sbuf_tensor_at(f"{name}_{self.uid}", list(shape), dt,
                                         offset=self.base + off)
        self.off = off + int(np.prod(shape[1:])) * _dtsize(dt)
        assert self.base + self.off <= self.top, \
            f"arena overflow at {name}: {self.base + self.off} > {self.top}"
        return t.ap()

    def reset(self):
        self.off = 0


def barrier(S):
    targets = {}
    for e in ENGINES:
        if S.cnt[e]:
            targets[e] = S.cnt[e]
        for ln in range(NLANE):
            k = f"{e}_dma{ln}"
            if S.lane_cnt[k]:
                targets[k] = S.lane_cnt[k]
    for e in ENGINES:
        fw = {k: v for k, v in targets.items()
              if k != e and S.floor.get((e, k), 0) < v}
        S.cnt[e] += 1
        inc = (S.sem[e], S.cnt[e], 1)
        S.steps.append((e, fw, (lambda eng: eng.nop()), inc))
        for k, v in fw.items():
            S.floor[(e, k)] = v
    for e in ENGINES:
        for k, v in targets.items():
            if k != e:
                S.floor[(e, k)] = max(S.floor.get((e, k), 0), v)
    S.hist.clear()


def bcast_like(src_ap, ref_ap):
    a, b = bass.broadcast_tensor_aps(src_ap, ref_ap)
    return a


def build_program(upto=9, dbg=()):
    nc = bass.Bass("TRN2", target_bir_lowering=False, detect_race_conditions=False)
    S = Sched(nc)
    PS = PsumPool(nc)

    inp = {}
    def din(name, shape, dtype=F32):
        inp[name] = nc.dram_tensor(name, list(shape), dtype, kind="ExternalInput").ap()
    din("x_fm", (D, T))
    din("wwT", (D, D)); din("wrT", (D, D)); din("wba", (D, 40))
    din("a_log", (1, 8)); din("dt_bias", (1, 8))
    for l in range(2):
        din(f"f_wqkv{l}", (D, 1536)); din(f"f_wo{l}", (512, D))
        din(f"f_w1a{l}", (D, 2816)); din(f"f_w1g{l}", (D, 2816))
        din(f"f_b1a{l}", (128, 22)); din(f"f_b1g{l}", (128, 22))
        din(f"f_w2{l}", (2816, D)); din(f"f_b2{l}", (128, 8))
        din(f"f_lng{l}", (128, 8)); din(f"f_lnb{l}", (128, 8))
        din(f"f_qs{l}", (128, 1)); din(f"f_ks{l}", (128, 1))
    din("a1_wqT", (D, D)); din("a1_wkT", (D, D)); din("a1_wvT", (D, D))
    din("a1_woT", (D, D)); din("a1_bo", (128, 8))
    din("g_wg", (2816, D)); din("g_wu", (2816, D)); din("g_wd", (D, 2816))
    for k, v in CONSTS.items():
        din(k, v.shape)

    out_fm = nc.dram_tensor("out_fm", [D, T], F32, kind="ExternalOutput").ap()
    dbg_aps = {}
    for name, shape in dbg:
        dbg_aps[name] = nc.dram_tensor("dbg_" + name, list(shape), F32,
                                       kind="ExternalOutput").ap()

    hbm = {}
    for nm, shape, dt in [("h_acc", (D, T), F32), ("h_xn", (D, T), F32),
                          ("h_xf", (D, T), F32), ("h_tmp", (D, T), F32),
                          ("h_otm", (T, D), BF16), ("h_xnb", (D, T), BF16),
                          ("h_q", (D, T), BF16), ("h_qr", (D, T), BF16),
                          ("h_k", (D, T), BF16), ("h_kr", (D, T), BF16),
                          ("h_v", (T, D), BF16), ("h_y", (D, T), BF16),
                          ("h_den", (128, 512), F32), ("h_a1", (D, T), F32),
                          ("h_xq", (D, T), BF16), ("h_h", (T, 512), BF16),
                          ("h_wgt", (D, 512), BF16), ("h_wdt", (512, D), BF16),
                          ("h_hq", (T, 512), BF16)]:
        hbm[nm] = nc.dram_tensor(nm, list(shape), dt, kind="Internal").ap()

    fixed = {}
    def sb(name, shape, dt=F32):
        fixed[name] = nc.alloc_sbuf_tensor(name, list(shape), dt).ap()
        return fixed[name]

    ctx = dict(nc=nc, S=S, PS=PS, inp=inp, hbm=hbm, out_fm=out_fm, dbg=dbg_aps,
               upto=upto)
    ctx["xnb"] = [sb(f"xnb{i}", (128, T + 1), BF16) for i in range(NCH)]
    ctx["ones1"] = sb("ones1", (1, 128))
    ctx["onescol"] = sb("onescol", (128, 1))
    ctx["onescol_b"] = sb("onescol_b", (128, 1), BF16)
    ctx["eye128"] = sb("eye128", (128, 128))
    ctx["eye128b"] = sb("eye128b", (128, 128), BF16)
    ctx["w32a"] = sb("w32a", (128, 512))
    ctx["w32b"] = sb("w32b", (128, 512))
    ctx["w32c"] = sb("w32c", (128, 512))
    ctx["w32d"] = sb("w32d", (128, 512))
    ctx["rowA"] = sb("rowA", (1, 512))
    ctx["rowB"] = sb("rowB", (1, 512))

    A = Arena(nc, nc.sbuf_base, nc.sbuf_top)
    ctx["A"] = A

    for nm, buf in [("c_ones1", ctx["ones1"]), ("c_onescol", ctx["onescol"]),
                    ("c_eye128", ctx["eye128"])]:
        S.dma("sync", buf, inp[nm])
    S.dma("gpsimd", ctx["onescol_b"], inp["c_onescol"])
    S.dma("gpsimd", ctx["eye128b"], inp["c_eye128"])

    emit_ln(ctx, src=inp["x_fm"], writeacc=True)
    barrier(S); A.reset()
    if upto >= 1:
        emit_deltanet(ctx)
        barrier(S); A.reset()
    if upto >= 2:
        emit_f0(ctx)
        barrier(S); A.reset()
    if upto >= 3:
        emit_ln(ctx, src=hbm["h_acc"], writeacc=False)
        barrier(S); A.reset()
    if upto >= 4:
        emit_l1_attn(ctx)
        barrier(S); A.reset()
    if upto >= 5:
        emit_l1_ffn(ctx)
        barrier(S); A.reset()
    emit_finish(ctx)

    with nc.Block() as blk:
        S.build(blk)
    nc.finalize()
    return nc


def emit_ln(ctx, src, writeacc):
    S = ctx["S"]; PS = ctx["PS"]; A = ctx["A"]; hbm = ctx["hbm"]
    xnb = ctx["xnb"]
    xt = [A.alloc("lnx", (128, 512)) for _ in range(2)]
    sq = ctx["w32a"]
    rme = A.alloc("ln_mu", (1, T))
    rr = A.alloc("ln_r", (1, T))
    rs2 = A.alloc("ln_s2", (1, T))
    re2 = A.alloc("ln_e2", (1, T))

    if writeacc:
        for i in range(NCH):
            S.memset(xnb[i][:, 0:1], 0.0)

    for j in range(8):
        sl = slice(j * 512, (j + 1) * 512)
        p1 = PS.get(); p2 = PS.get()
        for i in range(NCH):
            tile = xt[i % 2]
            S.dma("sync", tile, src[i * 128:(i + 1) * 128, sl])
            S.mm(p1[0:1, :], ctx["onescol"], tile, start=(i == 0), stop=(i == NCH - 1))
            S.act(ACT.Square, sq, tile)
            S.mm(p2[0:1, :], ctx["onescol"], sq, start=(i == 0), stop=(i == NCH - 1))
        S.ts(rme[:, sl], p1[0:1, :], 1.0 / D, None, OP.mult)
        S.ts(re2[:, sl], p2[0:1, :], 1.0 / D, None, OP.mult)
    S.tt(OP.mult, rs2, rme, rme)
    S.tt(OP.subtract, re2, re2, rs2)
    S.ts(re2, re2, 1e-5, None, OP.add)
    S.recip(re2, re2)
    S.act(ACT.Sqrt, rr, re2)
    S.tt(OP.mult, rs2, rme, rr)

    for j in range(8):
        sl = slice(j * 512, (j + 1) * 512)
        sl1 = slice(1 + j * 512, 1 + (j + 1) * 512)
        pr = PS.get(); pm = PS.get()
        S.mm(pr, ctx["ones1"], rr[0:1, sl])
        S.mm(pm, ctx["ones1"], rs2[0:1, sl])
        for i in range(NCH):
            tile = xt[i % 2]
            S.dma("sync", tile, src[i * 128:(i + 1) * 128, sl])
            wk = ctx["w32b"]
            S.tt(OP.mult, wk, tile, pr)
            S.tt(OP.subtract, wk, wk, pm)
            S.copy(xnb[i][:, sl1], wk)
            S.dma("gpsimd", hbm["h_xn"][i * 128:(i + 1) * 128, sl], wk)
            if writeacc:
                S.tt(OP.add, tile, tile, wk)
                S.dma("sync", hbm["h_acc"][i * 128:(i + 1) * 128, sl], tile)
            else:
                S.dma("sync", hbm["h_xnb"][i * 128:(i + 1) * 128, sl], xnb[i][:, sl1])


def emit_deltanet(ctx):
    S = ctx["S"]; PS = ctx["PS"]; A = ctx["A"]; inp = ctx["inp"]; hbm = ctx["hbm"]
    xnb = ctx["xnb"]; eye = ctx["eye128"]

    ww = [A.alloc("dnww", (128, 1024), BF16) for _ in range(NCH)]
    wba = [A.alloc("dnwba", (128, 40), BF16) for _ in range(NCH)]
    triu64 = A.alloc("dn_t64", (64, 64))
    ntriu64 = A.alloc("dn_nt64", (64, 64))
    stril64 = A.alloc("dn_s64", (64, 64))
    eye8 = A.alloc("dn_e8", (8, 8))
    maskD = A.alloc("dn_mD", (64, 512))
    maskDT = A.alloc("dn_mDT", (64, 512))
    for i in range(NCH):
        S.dma("gpsimd", ww[i], inp["wwT"][i * 128:(i + 1) * 128, :])
        S.dma("gpsimd", wba[i], inp["wba"][i * 128:(i + 1) * 128, :])
    for buf, nm in [(triu64, "c_triu64"), (ntriu64, "c_ntriu64"),
                    (stril64, "c_stril64"), (eye8, "c_eye8"),
                    (maskD, "c_maskD"), (maskDT, "c_maskDT")]:
        S.dma("sync", buf, inp[nm])

    acol = A.alloc("dn_acol", (8, 1))
    dtbcol = A.alloc("dn_dtb", (8, 1))
    bet = A.alloc("dn_bet", (8, T))
    ldf = A.alloc("dn_ldf", (8, 512))
    s8a = A.alloc("dn_s8a", (8, 512))
    s8b = A.alloc("dn_s8b", (8, 512))
    ld_tm = A.alloc("dn_ldtm", (64, 512))
    bet_tm = A.alloc("dn_bettm", (64, 512))
    cum = A.alloc("dn_cum", (64, 512))
    tail = A.alloc("dn_tail", (64, 512))
    de = A.alloc("dn_de", (64, 512))
    tailde = A.alloc("dn_tailde", (64, 512))
    nbde = A.alloc("dn_nbde", (64, 512))
    S.dma("sync", acol, inp["a_log"].rearrange("o e -> e o"))
    S.dma("sync", dtbcol, inp["dt_bias"].rearrange("o e -> e o"))
    S.act(ACT.Exp, acol, acol)
    S.ts(acol, acol, -1.0, None, OP.mult)
    for j in range(8):
        sl = slice(j * 512, (j + 1) * 512)
        pb = PS.get()
        S.mmgroup(pb[0:40, :], [(wba[i], xnb[i][:, 1 + j * 512:1 + (j + 1) * 512])
                                for i in range(NCH)])
        S.act(ACT.Sigmoid, bet[:, sl], pb[0:8, :])
        S.act(ACT.Abs, s8a, pb[32:40, :], bias=dtbcol)
        S.act(ACT.Exp, s8a, s8a, scale=-1.0)
        S.act(ACT.Ln, s8a, s8a, bias=1.0)
        S.act(ACT.Relu, s8b, pb[32:40, :], bias=dtbcol)
        S.tt(OP.add, s8a, s8a, s8b)
        S.ts(ldf, s8a, acol, None, OP.mult)
        for gg in range(8):
            g = j * 8 + gg
            c8 = slice(g * 8, (g + 1) * 8)
            pt = PS.get()
            S.trans(pt[0:64, 0:8], ldf[:, gg * 64:(gg + 1) * 64], eye8)
            S.trans(pt[0:64, 8:16], bet[:, g * 64:(g + 1) * 64], eye8)
            S.copy(ld_tm[0:64, c8], pt[0:64, 0:8])
            S.copy(bet_tm[0:64, c8], pt[0:64, 8:16])
            pc = PS.get()
            S.mm(pc[0:64, 0:8], triu64, ld_tm[0:64, c8])
            S.mm(pc[0:64, 8:16], stril64, ld_tm[0:64, c8])
            S.copy(cum[0:64, c8], pc[0:64, 0:8])
            S.copy(tail[0:64, c8], pc[0:64, 8:16])
    S.act(ACT.Exp, de, cum)
    S.act(ACT.Exp, tailde, tail)
    S.stt(nbde, bet_tm, -1.0, de, OP.mult, OP.mult)

    cedrow = ctx["rowA"]
    ced = A.alloc("dn_ced", (128, 512))
    pct = PS.get()
    S.mm(pct[0:1, :], ctx["onescol"][0:64, :], ld_tm)
    S.act(ACT.Exp, cedrow, pct[0:1, :])
    pc = PS.get()
    S.mm(pc, ctx["ones1"], cedrow)
    S.copy(ced, pc)

    vb = [A.alloc("dn_v", (64, 256), BF16) for _ in range(8)]
    ob = [A.alloc("dn_o", (64, 1024), BF16) for _ in range(8)]
    rk = [A.alloc("dn_rk", (128, 513), BF16) for _ in range(NCH)]
    ws = A.alloc("dn_ws", (128, 512), BF16)
    ldbc = A.alloc("dn_ldbc", (64, 512))
    Lf = A.alloc("dn_Lf", (64, 512))
    LTf = A.alloc("dn_LTf", (64, 512))
    Mb = A.alloc("dn_M", (64, 512), BF16)
    Malt = A.alloc("dn_Malt", (64, 512), BF16)
    Bk = [[[A.alloc("dn_B", (64, 512), BF16) for _ in range(6)] for _ in range(2)]]
    iT = [[A.alloc("dn_iT", (64, 512), BF16) for _ in range(2)]]
    wkt = [[A.alloc("dn_wkt", (64, 1024), BF16) for _ in range(2)]]
    Sst = [A.alloc("dn_S", (128, 128)) for _ in range(NCH)]
    Sbf = [A.alloc("dn_Sb", (128, 128), BF16) for _ in range(NCH)]
    ub = [A.alloc("dn_u", (64, 128), BF16) for _ in range(NCH)]
    ub2 = [A.alloc("dn_u2", (64, 128), BF16) for _ in range(NCH)]
    vsc = [A.alloc("dn_vsc", (64, 128), BF16) for _ in range(2)]
    row1 = ctx["rowB"]
    for h in range(NCH):
        S.memset(Sst[h], 0.0)
        S.memset(Sbf[h], 0.0)

    w32a, w32c = ctx["w32a"], ctx["w32c"]
    rowA, rowB = ctx["rowA"], ctx["rowB"]

    for b in range(8):
        bs = b * 512
        for h in range(NCH):
            if b == 0:
                S.memset(rk[h][:, 0:1], 0.0)
            else:
                S.copy(rk[h][:, 0:1], rk[h][:, 512:513])
            S.act(ACT.Square, w32a, xnb[h][:, 1 + bs:1 + bs + 512])
            pn = PS.get()
            S.mm(pn[0:1, :], ctx["onescol"], w32a)
            S.act(ACT.Sqrt, rowA, pn[0:1, :])
            S.ts(rowA, rowA, 1e-12, None, OP.max)
            S.recip(rowB, rowA)
            pbc = PS.get()
            S.mm(pbc, ctx["ones1"], rowB)
            S.tt(OP.mult, rk[h][:, 1:513], xnb[h][:, 1 + bs:1 + bs + 512], pbc)

        for p in range(4):
            par = 0
            for n in range(8):
                pv = PS.get()
                S.mmgroup(pv[0:64, 0:256],
                          [(xnb[i][:, 1 + bs + n * 64:1 + bs + (n + 1) * 64],
                            ww[i][:, p * 256:(p + 1) * 256]) for i in range(NCH)])
                S.copy(vb[n][0:64, :], pv[0:64, 0:256])
            for hh in range(2):
                h = 2 * p + hh
                S.dma("sync", row1, bet[h:h + 1, bs:bs + 512])
                pb = PS.get()
                S.mm(pb, ctx["ones1"], row1)
                S.tt(OP.mult, ws, rk[h][:, 0:512], pb)
                pPs = PS.get(); pPsT = PS.get(); pPre = PS.get()
                for n in range(8):
                    c = slice(n * 64, (n + 1) * 64)
                    wkc = rk[h][:, n * 64:n * 64 + 64]
                    rkc = rk[h][:, 1 + n * 64:1 + n * 64 + 64]
                    S.mm(pPs[0:64, c], ws[:, c], wkc)
                    S.mm(pPsT[0:64, c], wkc, ws[:, c])
                    S.mm(pPre[0:64, c], wkc, rkc)
                for n in range(8):
                    g = b * 8 + n
                    src = ld_tm[0:64, g * 8 + h:g * 8 + h + 1]
                    dst = ldbc[0:64, n * 64:(n + 1) * 64]
                    S.copy(dst, bcast_like(src, dst))
                pD = PS.get(); pDT = PS.get()
                for n in range(8):
                    c = slice(n * 64, (n + 1) * 64)
                    lc = ldbc[0:64, c]
                    S.mmgroup(pD[0:64, c], [(triu64, lc), (lc, ntriu64)])
                    S.mmgroup(pDT[0:64, c], [(lc, triu64), (ntriu64, lc)])
                S.tt(OP.add, Lf, pD[0:64, :], maskD)
                S.act(ACT.Exp, Lf, Lf)
                S.tt(OP.add, LTf, pDT[0:64, :], maskDT)
                S.act(ACT.Exp, LTf, LTf)
                S.stt(Mb, pPs[0:64, :], -1.0, Lf, OP.mult, OP.mult)
                S.stt(Bk[par][hh][0], pPsT[0:64, :], -1.0, LTf, OP.mult, OP.mult)
                S.stt(iT[par][hh], pPre[0:64, :], 1.0, LTf, OP.mult, OP.mult)

                def sq_pair(srcB, srcM, dstM, dstB):
                    pX = PS.get(); pY = PS.get()
                    for n in range(8):
                        c = slice(n * 64, (n + 1) * 64)
                        S.mm(pX[0:64, c], srcB[0:64, c], srcM[0:64, c])
                        S.mm(pY[0:64, c], srcM[0:64, c], srcB[0:64, c])
                    if dstM is not None:
                        S.copy(dstM, pX[0:64, :])
                    S.copy(dstB, pY[0:64, :])
                B = Bk[par][hh]
                sq_pair(B[0], Mb, Malt, B[1])
                sq_pair(B[1], Malt, Mb, B[2])
                sq_pair(B[2], Mb, Malt, B[3])
                sq_pair(B[3], Malt, Mb, B[4])
                sq_pair(B[4], Mb, None, B[5])
                for n in range(8):
                    g = b * 8 + n
                    pW = PS.get_bf()
                    S.trans(pW[0:64, 0:128], rk[h][:, n * 64:n * 64 + 64], ctx["eye128b"])
                    S.ts(wkt[par][hh][:, n * 128:(n + 1) * 128], pW[0:64, 0:128],
                         tailde[0:64, g * 8 + h:g * 8 + h + 1], None, OP.mult)
            for n in range(8):
                g = b * 8 + n
                for hh in range(2):
                    h = 2 * p + hh
                    jc = g * 8 + h
                    decol = de[0:64, jc:jc + 1]
                    nbcol = nbde[0:64, jc:jc + 1]
                    betcol = bet_tm[0:64, jc:jc + 1]
                    cedcol = ced[:, jc:jc + 1]
                    vcell = vb[n][0:64, hh * 128:(hh + 1) * 128]
                    ocell = ob[n][0:64, h * 128:(h + 1) * 128]
                    wkc = rk[h][:, n * 64:n * 64 + 64]
                    rkc = rk[h][:, 1 + n * 64:1 + n * 64 + 64]
                    S.ts(vsc[hh], vcell, betcol, None, OP.mult)
                    pw = PS.get()
                    S.mm(pw[0:64, 0:128], wkc, Sbf[h])
                    S.stt(ub[h], pw[0:64, 0:128], nbcol, vsc[hh], OP.mult, OP.add)
                    usrc, udst = ub[h], ub2[h]
                    for k in range(6):
                        pk = PS.get()
                        S.mm(pk[0:64, 0:128], Bk[par][hh][k][0:64, n * 64:(n + 1) * 64], usrc)
                        S.tt(OP.add, udst, usrc, pk[0:64, 0:128])
                        usrc, udst = udst, usrc
                    vn = usrc
                    po = PS.get()
                    S.mm(po[0:64, 0:128], rkc, Sbf[h])
                    S.ts(w32c[0:64, 0:128], po[0:64, 0:128], decol, None, OP.mult)
                    pi = PS.get()
                    S.mm(pi[0:64, 0:128], iT[par][hh][0:64, n * 64:(n + 1) * 64], vn)
                    S.tt(OP.add, ocell, w32c[0:64, 0:128], pi[0:64, 0:128])
                    pS = PS.get()
                    S.mm(pS[:, 0:128], wkt[par][hh][:, n * 128:(n + 1) * 128], vn)
                    S.stt(Sst[h], Sst[h], cedcol, pS[:, 0:128], OP.mult, OP.add)
                    S.copy(Sbf[h], Sst[h])
        for n in range(8):
            S.dma("sync", hbm["h_otm"][bs + n * 64:bs + (n + 1) * 64, :], ob[n][0:64, :])

    barrier(S)
    A.reset()
    wr = [A.alloc("dn_wr", (128, 1024), BF16) for _ in range(NCH)]
    for i in range(NCH):
        S.dma("gpsimd", wr[i], inp["wrT"][i * 128:(i + 1) * 128, :])
    of = [A.alloc("dn_of", (128, T), BF16) for _ in range(NCH)]
    ev = [A.alloc("dn_ev", (128, 512)) for _ in range(2)]
    for i in range(NCH):
        S.dmaT("sync", of[i], hbm["h_otm"][:, i * 128:(i + 1) * 128])
    for ip in range(NCH):
        for j in range(8):
            pp = PS.get()
            S.mmgroup(pp, [(wr[e][:, ip * 128:(ip + 1) * 128],
                            of[e][:, j * 512:(j + 1) * 512]) for e in range(NCH)])
            t = ev[j % 2]
            S.copy(t, pp)
            S.dma("gpsimd", hbm["h_acc"][ip * 128:(ip + 1) * 128, j * 512:(j + 1) * 512],
                  t, accum_op=OP.add)


def emit_f0(ctx):
    S = ctx["S"]; PS = ctx["PS"]; A = ctx["A"]; inp = ctx["inp"]; hbm = ctx["hbm"]
    xnb = ctx["xnb"]
    w32a, w32b, w32c, w32d = ctx["w32a"], ctx["w32b"], ctx["w32c"], ctx["w32d"]

    for l in range(2):
        src = hbm["h_xn"] if l == 0 else hbm["h_tmp"]
        mid = hbm["h_xf"]
        fwq = [A.alloc("f_wq", (128, 1536), BF16) for _ in range(NCH)]
        fwo = [A.alloc("f_wo", (128, 1024), BF16) for _ in range(4)]
        for i in range(NCH):
            S.dma("gpsimd", fwq[i], inp[f"f_wqkv{l}"][i * 128:(i + 1) * 128, :])
        for oc in range(4):
            S.dma("gpsimd", fwo[oc], inp[f"f_wo{l}"][oc * 128:(oc + 1) * 128, :])
        qscol = A.alloc("f_qs", (128, 1)); kscol = A.alloc("f_ks", (128, 1))
        S.dma("sync", qscol, inp[f"f_qs{l}"])
        S.dma("sync", kscol, inp[f"f_ks{l}"])
        hsel = A.alloc("f_hsel", (2, 128)); hselT = A.alloc("f_hselT", (128, 2))
        S.dma("sync", hsel, inp["c_halfsel"])
        S.dma("sync", hselT, inp["c_halfselT"])
        permx = A.alloc("f_permx", (128, 128))
        S.dma("sync", permx, inp["c_perm_xpos"])
        mB = [A.alloc("f_mB", (128, 512), BF16) for _ in range(8)]
        for b in range(8):
            S.dma("gpsimd", mB[b], inp[f"c_la_mB{b}"])
        tbl = {nm: A.alloc("f_tb", (128, 512)) for nm in ["cq", "sq", "ck", "sk"]}
        fq = [A.alloc("f_q", (128, 512), BF16) for _ in range(4)]
        fk = [[A.alloc("f_k", (128, 512), BF16) for _ in range(4)] for _ in range(2)]
        fv = [[A.alloc("f_v", (128, 512), BF16) for _ in range(4)] for _ in range(2)]
        fo = [A.alloc("f_o", (128, 512), BF16) for _ in range(4)]
        fE = [A.alloc("f_E", (128, 512), BF16) for _ in range(2)]
        row2a = A.alloc("f_r2a", (2, 512)); negm = [A.alloc("f_negm", (128, 4)) for _ in range(8)]
        negmrow = [A.alloc("f_nmr", (1, 512)) for _ in range(8)]
        fdr = A.alloc("f_dr", (1, 512))
        xt = [A.alloc("f_xt", (128, 512)) for _ in range(2)]

        def xsh(i, t0, n):
            off = 1 if i < 4 else 0
            return xnb[i][:, off + t0: off + t0 + n]

        for w in range(8):
            ring, ringp = w % 2, (w + 1) % 2
            for nm, cn in [("cq", "c_xpos_cq"), ("sq", "c_xpos_sq"),
                           ("ck", "c_xpos_ck"), ("sk", "c_xpos_sk")]:
                S.dma("sync", tbl[nm], inp[cn][:, w * 512:(w + 1) * 512])
            for c in range(4):
                for which, dst, scol, ct, st in [
                        (0, fq[c], qscol, tbl["cq"], tbl["sq"]),
                        (1, fk[ring][c], kscol, tbl["ck"], tbl["sk"])]:
                    pq = PS.get()
                    S.mmgroup(pq, [(fwq[i][:, which * 512 + c * 128: which * 512 + (c + 1) * 128],
                                    xsh(i, w * 512, 512)) for i in range(NCH)])
                    S.act(ACT.Square, w32a, pq)
                    pn2 = PS.get()
                    S.mm(pn2[0:2, :], hselT, w32a)
                    S.act(ACT.Sqrt, row2a, pn2[0:2, :])
                    S.ts(row2a, row2a, 1e-12, None, OP.max)
                    S.recip(row2a, row2a)
                    pbc = PS.get()
                    S.mm(pbc, hsel, row2a)
                    S.ts(w32b, pq, scol, None, OP.mult)
                    S.tt(OP.mult, w32b, w32b, pbc)
                    pr = PS.get()
                    S.mm(pr, permx, w32b)
                    S.tt(OP.mult, w32c, w32b, ct)
                    S.tt(OP.mult, w32d, pr, st)
                    S.tt(OP.add, dst, w32c, w32d)
            for t4 in range(4):
                pv = PS.get()
                S.mmgroup(pv, [(xsh(i, w * 512 + t4 * 128, 128), fwq[i][:, 1024:1536])
                               for i in range(NCH)])
                S.copy(fv[ring][t4], pv)
            for h in range(8):
                qc, base = h // 2, (h % 2) * 64
                for c4 in range(4):
                    lhsT = fq[qc][base:base + 64, c4 * 128:(c4 + 1) * 128]
                    lo = c4 * 128
                    pA2 = PS.get()
                    S.mm(pA2, lhsT, fk[ring][qc][base:base + 64, :])
                    if w > 0:
                        pA = PS.get()
                        S.mm(pA, lhsT, fk[ringp][qc][base:base + 64, :])
                        S.red(OP.max, w32a[:, 0:1], pA[:, lo:512])
                        S.red(OP.max, w32a[:, 1:2], pA2[:, 0:lo + 128])
                        S.tt(OP.max, w32a[:, 2:3], w32a[:, 0:1], w32a[:, 1:2])
                        S.ts(negm[h][:, c4:c4 + 1], w32a[:, 2:3], -1.0, None, OP.mult)
                    else:
                        S.red(OP.max, w32a[:, 0:1], pA2[:, 0:lo + 128])
                        S.ts(negm[h][:, c4:c4 + 1], w32a[:, 0:1], -1.0, None, OP.mult)
                pnr = PS.get()
                for c4 in range(4):
                    S.trans(pnr[0:1, c4 * 128:(c4 + 1) * 128], negm[h][:, c4:c4 + 1],
                            ctx["eye128"])
                S.copy(negmrow[h], pnr[0:1, :])
                blocks = list(range(8)) if w > 0 else list(range(4, 8))
                pDen = PS.resv(0); pAV = PS.resv(1)
                for idx, bb in enumerate(blocks):
                    rr2 = ringp if bb < 4 else ring
                    kb = bb % 4
                    pB = PS.get()
                    S.mm(pB, fk[rr2][qc][base:base + 64, kb * 128:(kb + 1) * 128],
                         fq[qc][base:base + 64, :], start=True, stop=False)
                    S.mm(pB, ctx["ones1"], negmrow[h], start=False, stop=True)
                    E = fE[idx % 2]
                    S.act(ACT.Exp, E, pB, scale=8.0)
                    S.tt(OP.mult, E, E, mB[bb])
                    S.mm(pDen[0:1, :], ctx["onescol_b"], E,
                         start=(idx == 0), stop=(idx == len(blocks) - 1))
                    S.mm(pAV[base:base + 64, :], fv[rr2][kb][:, h * 64:(h + 1) * 64], E,
                         start=(idx == 0), stop=(idx == len(blocks) - 1),
                         tp=(0, base))
                S.recip(fdr, pDen[0:1, :])
                pbcD = PS.get()
                S.mm(pbcD, ctx["ones1"], fdr)
                S.copy(w32c[base:base + 64, :], pAV[base:base + 64, :])
                S.tt(OP.mult, fo[qc][base:base + 64, :], w32c[base:base + 64, :],
                     pbcD[base:base + 64, :])
            for ip in range(NCH):
                pP = PS.get()
                S.mmgroup(pP, [(fwo[oc][:, ip * 128:(ip + 1) * 128], fo[oc])
                               for oc in range(4)])
                t = xt[ip % 2]
                S.dma("sync", t, src[ip * 128:(ip + 1) * 128, w * 512:(w + 1) * 512])
                S.tt(OP.add, t, t, pP)
                S.dma("sync", mid[ip * 128:(ip + 1) * 128, w * 512:(w + 1) * 512], t)

        barrier(S); A.reset()
        w1a = [A.alloc("f_w1a", (128, 11264), BF16) for _ in range(2)]
        w1g = [A.alloc("f_w1g", (128, 1024), BF16) for _ in range(2)]
        for oc in range(22):
            hf, off = oc // 11, (oc % 11) * 1024
            S.dma("gpsimd",
                  w1a[hf][:, off:off + 1024].rearrange("p (i c) -> p i c", c=128),
                  inp[f"f_w1a{l}"][:, oc * 128:(oc + 1) * 128]
                  .rearrange("(i p) c -> p i c", p=128))
        b1a = A.alloc("f_b1a", (128, 22)); b1g = A.alloc("f_b1g", (128, 22))
        b2 = A.alloc("f_b2", (128, 8))
        lng = A.alloc("f_lng", (128, 8)); lnb = A.alloc("f_lnb", (128, 8))
        for buf, nm in [(b1a, f"f_b1a{l}"), (b1g, f"f_b1g{l}"), (b2, f"f_b2{l}"),
                        (lng, f"f_lng{l}"), (lnb, f"f_lnb{l}")]:
            S.dma("sync", buf, inp[nm])
        fmi = [A.alloc("f_mi", (128, 512), BF16) for _ in range(NCH)]
        fhh = [A.alloc("f_hh", (128, 512), BF16) for _ in range(22)]
        fw2 = [A.alloc("f_w2", (128, 2816), BF16) for _ in range(2)]
        xt = [A.alloc("f_mxt", (128, 513)) for _ in range(2)]
        xcur = [A.alloc("f_mxc", (128, 512)) for _ in range(2)]
        ev = [A.alloc("f_mev", (128, 512)) for _ in range(2)]
        r1 = A.alloc("f_mr1", (1, 512)); r2 = A.alloc("f_mr2", (1, 512))
        r3 = A.alloc("f_mr3", (1, 512))

        for j in range(8):
            t0 = j * 512
            psS = PS.get(); psQ = PS.get()
            for i in range(NCH):
                t = xt[i % 2]
                if j == 0:
                    S.memset(t[:, 0:1], 0.0)
                    S.dma("sync", t[:, 1:513], mid[i * 128:(i + 1) * 128, 0:512])
                else:
                    S.dma("sync", t, mid[i * 128:(i + 1) * 128, t0 - 1:t0 + 512])
                sh = t[:, 1:513] if i < 4 else t[:, 0:512]
                S.mm(psS[0:1, :], ctx["onescol"], sh, start=(i == 0), stop=(i == NCH - 1))
                S.act(ACT.Square, w32a, sh)
                S.mm(psQ[0:1, :], ctx["onescol"], w32a, start=(i == 0), stop=(i == NCH - 1))
            S.ts(r1, psS[0:1, :], 1.0 / D, None, OP.mult)
            S.ts(r2, psQ[0:1, :], 1.0 / D, None, OP.mult)
            S.tt(OP.mult, r3, r1, r1)
            S.tt(OP.subtract, r2, r2, r3)
            S.ts(r2, r2, 1e-5, None, OP.add)
            S.recip(r2, r2)
            S.act(ACT.Sqrt, r2, r2)
            S.tt(OP.mult, r3, r1, r2)
            pbcR = PS.get(); pbcM = PS.get()
            S.mm(pbcR, ctx["ones1"], r2)
            S.mm(pbcM, ctx["ones1"], r3)
            for i in range(NCH):
                t = xt[i % 2]
                if j == 0:
                    S.memset(t[:, 0:1], 0.0)
                    S.dma("sync", t[:, 1:513], mid[i * 128:(i + 1) * 128, 0:512])
                else:
                    S.dma("sync", t, mid[i * 128:(i + 1) * 128, t0 - 1:t0 + 512])
                sh = t[:, 1:513] if i < 4 else t[:, 0:512]
                S.tt(OP.mult, w32b, sh, pbcR)
                S.tt(OP.subtract, w32b, w32b, pbcM)
                S.ts(fmi[i], w32b, lng[:, i:i + 1], lnb[:, i:i + 1], OP.mult, OP.add)
            for oc in range(22):
                hf, off = oc // 11, (oc % 11) * 1024
                g2 = w1g[oc % 2]
                S.dma("gpsimd", g2.rearrange("p (i c) -> p i c", c=128),
                      inp[f"f_w1g{l}"][:, oc * 128:(oc + 1) * 128]
                      .rearrange("(i p) c -> p i c", p=128))
                psA2 = PS.get(); psG = PS.get()
                S.mmgroup(psA2, [(w1a[hf][:, off + i * 128: off + (i + 1) * 128], fmi[i])
                                 for i in range(NCH)])
                S.mmgroup(psG, [(g2[:, i * 128:(i + 1) * 128], fmi[i])
                                for i in range(NCH)])
                S.act(ACT.Gelu, w32a, psG, bias=b1g[:, oc:oc + 1])
                S.stt(fhh[oc], psA2, b1a[:, oc:oc + 1], w32a, OP.add, OP.mult)
            for ip in range(NCH):
                f2 = fw2[ip % 2]
                S.dma("gpsimd", f2.rearrange("p (m c) -> p m c", c=128),
                      inp[f"f_w2{l}"][:, ip * 128:(ip + 1) * 128]
                      .rearrange("(m p) c -> p m c", p=128))
                pO = PS.get()
                S.mmgroup(pO, [(f2[:, mc * 128:(mc + 1) * 128], fhh[mc]) for mc in range(22)])
                t = ev[ip % 2]
                S.ts(t, pO, b2[:, ip:ip + 1], None, OP.add)
                cu = xcur[ip % 2]
                S.dma("sync", cu, mid[ip * 128:(ip + 1) * 128, t0:t0 + 512])
                S.tt(OP.add, t, t, cu)
                if l == 0:
                    S.copy(xnb[ip][:, 1 + t0:1 + t0 + 512], t)
                    S.dma("sync", hbm["h_tmp"][ip * 128:(ip + 1) * 128, t0:t0 + 512], t)
                else:
                    S.dma("gpsimd", hbm["h_acc"][ip * 128:(ip + 1) * 128, t0:t0 + 512],
                          t, accum_op=OP.add)
        barrier(S); A.reset()


def emit_l1_attn(ctx):
    S = ctx["S"]; PS = ctx["PS"]; A = ctx["A"]; inp = ctx["inp"]; hbm = ctx["hbm"]
    xnb = ctx["xnb"]
    w32b, w32c = ctx["w32b"], ctx["w32c"]

    wq = [A.alloc("a_wq", (128, 1024), BF16) for _ in range(NCH)]
    wk = [A.alloc("a_wk", (128, 1024), BF16) for _ in range(NCH)]
    wv = [A.alloc("a_wv", (128, 1024), BF16) for _ in range(NCH)]
    for i in range(NCH):
        S.dma("gpsimd", wq[i], inp["a1_wqT"][i * 128:(i + 1) * 128, :])
        S.dma("gpsimd", wk[i], inp["a1_wkT"][i * 128:(i + 1) * 128, :])
        S.dma("gpsimd", wv[i], inp["a1_wvT"][i * 128:(i + 1) * 128, :])
    permr = A.alloc("a_perm", (128, 128), BF16)
    S.dma("gpsimd", permr, inp["c_perm_rope"])
    rct = A.alloc("a_rc", (128, 512))
    rst = A.alloc("a_rs", (128, 512))
    tq = [A.alloc("a_tq", (128, 512), BF16) for _ in range(2)]
    tqr = [A.alloc("a_tqr", (128, 512), BF16) for _ in range(2)]
    for j in range(8):
        sl = slice(j * 512, (j + 1) * 512)
        S.dma("sync", rct, inp["c_rope_c"][:, sl])
        S.dma("sync", rst, inp["c_rope_s"][:, sl])
        for ch in range(NCH):
            for wi, (wsb, hq, hqr) in enumerate([(wq, "h_q", "h_qr"), (wk, "h_k", "h_kr")]):
                pq = PS.get()
                S.mmgroup(pq, [(wsb[i][:, ch * 128:(ch + 1) * 128],
                                xnb[i][:, 1 + j * 512:1 + (j + 1) * 512])
                               for i in range(NCH)])
                S.copy(tq[wi], pq)
                S.dma("sync", hbm[hq][ch * 128:(ch + 1) * 128, sl], tq[wi])
                pr = PS.get()
                S.mm(pr, permr, tq[wi])
                S.tt(OP.mult, w32b, tq[wi], rct)
                S.tt(OP.mult, w32c, pr, rst)
                S.tt(OP.add, tqr[wi], w32b, w32c)
                S.dma("sync", hbm[hqr][ch * 128:(ch + 1) * 128, sl], tqr[wi])
        for t4 in range(4):
            ts0 = j * 512 + t4 * 128
            for eh in range(2):
                pv = PS.get()
                S.mmgroup(pv, [(xnb[i][:, 1 + ts0:1 + ts0 + 128],
                                wv[i][:, eh * 512:(eh + 1) * 512]) for i in range(NCH)])
                S.copy(tq[eh], pv)
                S.dma("sync", hbm["h_v"][ts0:ts0 + 128, eh * 512:(eh + 1) * 512], tq[eh])

    barrier(S); A.reset()
    wo = [A.alloc("a_wo", (128, 1024), BF16) for _ in range(NCH)]
    for i in range(NCH):
        S.dma("gpsimd", wo[i], inp["a1_woT"][i * 128:(i + 1) * 128, :])
    bo = A.alloc("a_bo", (128, 8))
    S.dma("sync", bo, inp["a1_bo"])
    dmask = [A.alloc("a_dm", (128, 512), BF16) for _ in range(4)]
    for b in range(4):
        S.dma("gpsimd", dmask[b], inp[f"c_l1diag{b}"])
    aq = [A.alloc("a_q", (128, 512), BF16) for _ in range(NCH)]
    aqr = [A.alloc("a_qr", (128, 512), BF16) for _ in range(NCH)]
    ak = [A.alloc("a_k", (128, 512), BF16) for _ in range(NCH)]
    akr = [A.alloc("a_kr", (128, 512), BF16) for _ in range(NCH)]
    av = [A.alloc("a_v", (128, 1024), BF16) for _ in range(4)]
    aE = [A.alloc("a_E", (128, 512), BF16) for _ in range(2)]
    dscr = A.alloc("a_ds", (128, 512))
    negc = A.alloc("a_negc", (128, 1))
    S.memset(negc, -50.0)
    dp = [A.alloc("a_dp", (128, 32)) for _ in range(16)]
    aden = [A.alloc("a_den", (128, 4)) for _ in range(16)]
    aden2 = [A.alloc("a_den2", (128, 4)) for _ in range(16)]
    adr = [A.alloc("a_drow", (1, 512)) for _ in range(16)]
    yac = [A.alloc("a_yac", (128, 512)) for _ in range(NCH)]
    ytile = [A.alloc("a_yt", (128, 512), BF16) for _ in range(NCH)]
    ev = [A.alloc("a_ev", (128, 512)) for _ in range(2)]

    for jq in range(8):
        qsl = slice(jq * 512, (jq + 1) * 512)
        for ch in range(NCH):
            S.dma("sync", aq[ch], hbm["h_q"][ch * 128:(ch + 1) * 128, qsl])
            S.dma("sync", aqr[ch], hbm["h_qr"][ch * 128:(ch + 1) * 128, qsl])
        for hp in range(NCH):
            S.memset(yac[hp], 0.0)
        for tks in range(8):
            ksl = slice(tks * 512, (tks + 1) * 512)
            for ch in range(NCH):
                S.dma("sync", ak[ch], hbm["h_k"][ch * 128:(ch + 1) * 128, ksl])
                if tks <= jq:
                    S.dma("sync", akr[ch], hbm["h_kr"][ch * 128:(ch + 1) * 128, ksl])
            if tks <= jq:
                for t4 in range(4):
                    S.dma("sync", av[t4],
                          hbm["h_v"][tks * 512 + t4 * 128:tks * 512 + (t4 + 1) * 128, :])
            for h in range(16):
                hp, base = h // 2, (h % 2) * 64
                for c4 in range(4):
                    pD = PS.get()
                    S.mm(pD, aq[hp][base:base + 64, c4 * 128:(c4 + 1) * 128],
                         ak[hp][base:base + 64, :])
                    S.act(ACT.Exp, dscr, pD, scale=0.125, bias=negc,
                          accum_out=dp[h][:, tks * 4 + c4:tks * 4 + c4 + 1])
                if tks <= jq:
                    pAV = PS.resv(0)
                    for b4 in range(4):
                        pN = PS.get()
                        S.mm(pN, akr[hp][base:base + 64, b4 * 128:(b4 + 1) * 128],
                             aqr[hp][base:base + 64, :])
                        E = aE[b4 % 2]
                        S.act(ACT.Exp, E, pN, scale=0.125, bias=negc)
                        if tks == jq:
                            S.tt(OP.mult, E, E, dmask[b4])
                        S.mm(pAV[base:base + 64, :],
                             av[b4][:, h * 64:(h + 1) * 64], E,
                             start=(b4 == 0), stop=(b4 == 3), tp=(0, base))
                    S.tt(OP.add, yac[hp][base:base + 64, :], yac[hp][base:base + 64, :],
                         pAV[base:base + 64, :])
        for h in range(16):
            hp, base = h // 2, (h % 2) * 64
            S.red(OP.add, aden[h], dp[h].rearrange("p (a c) -> p c a", c=4))
            if "h_den" in ctx["dbg"]:
                S.dma("sync", hbm["h_den"][:, (jq * 16 + h) * 4:(jq * 16 + h) * 4 + 4],
                      aden[h])
            prow = PS.get()
            for c4 in range(4):
                S.mm(prow[0:1, c4 * 128:(c4 + 1) * 128], aden[h][:, c4:c4 + 1],
                     ctx["eye128"])
            S.copy(aden2[h][0:1, 0:1], prow[0:1, 0:1])
            S.copy(adr[h], prow[0:1, :])
            S.recip(adr[h], adr[h])
            pbc = PS.get()
            S.mm(pbc, ctx["ones1"], adr[h])
            S.tt(OP.mult, ytile[hp][base:base + 64, :], yac[hp][base:base + 64, :],
                 pbc[base:base + 64, :])
        if "h_y" in ctx["dbg"]:
            for hp in range(NCH):
                S.dma("sync", hbm["h_y"][hp * 128:(hp + 1) * 128, qsl], ytile[hp])
        for ip in range(NCH):
            pP = PS.get()
            S.mmgroup(pP, [(wo[e][:, ip * 128:(ip + 1) * 128], ytile[e])
                           for e in range(NCH)])
            t = ev[ip % 2]
            S.ts(t, pP, bo[:, ip:ip + 1], None, OP.add)
            if "h_a1" in ctx["dbg"]:
                S.dma("sync", hbm["h_a1"][ip * 128:(ip + 1) * 128, qsl], t)
            S.dma("gpsimd", hbm["h_acc"][ip * 128:(ip + 1) * 128, qsl], t,
                  accum_op=OP.add)


def emit_quant(ctx, out_bf, in_ap, W, scr):
    S = ctx["S"]
    GRID = [0.0, 0.5, 1.0, 1.5, 2.0, 3.0, 4.0, 6.0]
    MID = [0.25, 0.75, 1.25, 1.75, 2.5, 3.5, 5.0]
    in3 = in_ap.rearrange("p (a b) -> p a b", b=16)
    S.red(OP.max, scr["qs"], in3, absval=True)
    S.act(ACT.Abs, scr["ax"], in_ap)
    S.act(ACT.Sign, scr["sg"], in_ap)
    se3 = scr["se"].rearrange("p (a b) -> p a b", b=16)
    qsv = scr["qs"].rearrange("p (a b) -> p a b", b=1)
    for b in range(16):
        S.copy(se3[:, :, b:b + 1], qsv)
    S.ts(scr["se"], scr["se"], 1.0 / 6.0, None, OP.mult)
    S.ts(scr["se"], scr["se"], 1e-12, None, OP.max)
    first = True
    for i in range(7):
        S.stt(scr["cm"], scr["se"], MID[i], scr["ax"], OP.mult, OP.is_lt)
        d = GRID[i + 1] - GRID[i]
        if first:
            S.ts(scr["acc"], scr["cm"], d, None, OP.mult)
            first = False
        else:
            S.stt(scr["acc"], scr["cm"], d, scr["acc"], OP.mult, OP.add)
    S.tt(OP.mult, scr["acc"], scr["acc"], scr["se"])
    S.tt(OP.mult, out_bf, scr["acc"], scr["sg"])


def emit_l1_ffn(ctx):
    S = ctx["S"]; PS = ctx["PS"]; A = ctx["A"]; inp = ctx["inp"]; hbm = ctx["hbm"]
    eye = ctx["eye128b"]
    w32a = ctx["w32a"]

    scr = {"qs": A.alloc("q_qs", (128, 64)), "th": A.alloc("q_th", (128, 64))}
    for nm in ["ax", "se", "cm", "acc", "sg"]:
        scr[nm] = A.alloc("q_" + nm, (128, 1024))
    def scr_w(W):
        d = {k: scr[k][:, 0:W // 16] for k in ("qs", "th")}
        d.update({k: scr[k][:, 0:W] for k in ("ax", "se", "cm", "acc", "sg")})
        return d

    xq = [A.alloc("g_xq", (128, T), BF16) for _ in range(NCH)]
    xtile = A.alloc("g_xt", (128, 1024))
    xftmp = [ctx["w32b"][:, 0:128], ctx["w32c"][:, 0:128]]
    xqt = A.alloc("g_xqt", (128, 1024), BF16)
    for tc in range(32):
        for i in range(NCH):
            t = xftmp[i % 2]
            S.dma("sync", t, hbm["h_xn"][i * 128:(i + 1) * 128, tc * 128:(tc + 1) * 128])
            pt = PS.get()
            S.trans(pt[:, 0:128], t, ctx["eye128"])
            S.copy(xtile[:, i * 128:(i + 1) * 128], pt[:, 0:128])
        emit_quant(ctx, xqt, xtile, 1024, scr_w(1024))
        for i in range(NCH):
            pt = PS.get_bf()
            S.trans(pt[:, 0:128], xqt[:, i * 128:(i + 1) * 128], eye)
            S.copy(xq[i][:, tc * 128:(tc + 1) * 128], pt[:, 0:128])
    if "h_xq" in ctx["dbg"]:
        for i in range(NCH):
            S.dma("sync", hbm["h_xq"][i * 128:(i + 1) * 128, :], xq[i])

    NB = [(0, 512), (512, 512), (1024, 512), (1536, 512), (2048, 512), (2560, 256)]
    wnat = A.alloc("g_wn", (128, 1024))
    wnq = A.alloc("g_wnq", (128, 1024), BF16)
    wdnat = A.alloc("g_wdn", (128, 512))
    wdq = A.alloc("g_wdq", (128, 512), BF16)
    wgT = [A.alloc("g_wgT", (128, 512), BF16) for _ in range(NCH)]
    wuT = [A.alloc("g_wuT", (128, 512), BF16) for _ in range(NCH)]
    wdT = [A.alloc("g_wdT", (128, 1024), BF16) for _ in range(4)]
    hrow = A.alloc("g_h", (128, 512))
    hq = A.alloc("g_hq", (128, 512), BF16)
    hqT = [A.alloc("g_hqT", (128, 128), BF16) for _ in range(4)]

    for (n0, nw) in NB:
        nchunks = nw // 128
        for wname, wT in [("g_wg", wgT), ("g_wu", wuT)]:
            for ncc in range(nchunks):
                S.dma("sync", wnat,
                      inp[wname][n0 + ncc * 128:n0 + (ncc + 1) * 128, :])
                emit_quant(ctx, wnq, wnat, 1024, scr_w(1024))
                for i in range(NCH):
                    pt = PS.get_bf()
                    S.trans(pt[:, 0:128], wnq[:, i * 128:(i + 1) * 128], eye)
                    S.copy(wT[i][:, ncc * 128:(ncc + 1) * 128], pt[:, 0:128])
        if n0 == 0 and "h_wgt" in ctx["dbg"]:
            for i in range(NCH):
                S.dma("sync", hbm["h_wgt"][i * 128:(i + 1) * 128, :], wgT[i])
        for i2 in range(NCH):
            S.dma("sync", wdnat[:, 0:nw],
                  inp["g_wd"][i2 * 128:(i2 + 1) * 128, n0:n0 + nw])
            emit_quant(ctx, wdq[:, 0:nw], wdnat[:, 0:nw], nw, scr_w(nw))
            for ncc in range(nchunks):
                pt = PS.get_bf()
                S.trans(pt[:, 0:128], wdq[:, ncc * 128:(ncc + 1) * 128], eye)
                S.copy(wdT[ncc][:, i2 * 128:(i2 + 1) * 128], pt[:, 0:128])
        if n0 == 0 and "h_wdt" in ctx["dbg"]:
            for ncc in range(4):
                S.dma("sync", hbm["h_wdt"][ncc * 128:(ncc + 1) * 128, :], wdT[ncc])
        for tc in range(32):
            pG = PS.get()
            S.mmgroup(pG[:, 0:nw], [(xq[i][:, tc * 128:(tc + 1) * 128], wgT[i][:, 0:nw])
                                    for i in range(NCH)])
            pU = PS.get()
            S.mmgroup(pU[:, 0:nw], [(xq[i][:, tc * 128:(tc + 1) * 128], wuT[i][:, 0:nw])
                                    for i in range(NCH)])
            S.act(ACT.Silu, w32a[:, 0:nw], pG[:, 0:nw])
            S.tt(OP.mult, hrow[:, 0:nw], w32a[:, 0:nw], pU[:, 0:nw])
            if n0 == 0 and "h_h" in ctx["dbg"]:
                S.dma("gpsimd", hbm["h_h"][tc * 128:(tc + 1) * 128, :], hrow[:, 0:512])
            emit_quant(ctx, hq[:, 0:nw], hrow[:, 0:nw], nw, scr_w(nw))
            if n0 == 0 and "h_hq" in ctx["dbg"]:
                S.dma("sync", hbm["h_hq"][tc * 128:(tc + 1) * 128, :], hq[:, 0:512])
            for ncc in range(nchunks):
                pt = PS.get_bf()
                S.trans(pt[:, 0:128], hq[:, ncc * 128:(ncc + 1) * 128], eye)
                S.copy(hqT[ncc], pt[:, 0:128])
            gah = [ctx["w32c"], ctx["w32d"]]
            for i2 in range(NCH):
                pDn = PS.get()
                S.mmgroup(pDn[:, 0:128], [(wdT[ncc][:, i2 * 128:(i2 + 1) * 128], hqT[ncc])
                                          for ncc in range(nchunks)])
                S.copy(gah[i2 // 4][:, (i2 % 4) * 128:(i2 % 4 + 1) * 128], pDn[:, 0:128])
            for a2 in range(NCH):
                S.dma("gpsimd",
                      hbm["h_acc"][a2 * 128:(a2 + 1) * 128, tc * 128:(tc + 1) * 128],
                      gah[a2 // 4][:, (a2 % 4) * 128:(a2 % 4 + 1) * 128], accum_op=OP.add)


def emit_finish(ctx):
    S = ctx["S"]; A = ctx["A"]; hbm = ctx["hbm"]
    barrier(S)
    A.reset()
    t = [A.alloc("fin_t", (128, T)) for _ in range(2)]
    for i in range(NCH):
        S.dma("sync", t[i % 2], hbm["h_acc"][i * 128:(i + 1) * 128, :])
        S.dma("sync", ctx["out_fm"][i * 128:(i + 1) * 128, :], t[i % 2])
    tb = A.alloc("fin_tb", (128, T), BF16)
    for name, ap in ctx["dbg"].items():
        if name not in hbm:
            continue
        srct = hbm[name]
        if srct.tensor.dtype == F32:
            for i in range(srct.shape[0] // 128):
                w = srct.shape[1]
                S.dma("sync", t[i % 2][:, 0:w], srct[i * 128:(i + 1) * 128, :])
                S.dma("sync", ap[i * 128:(i + 1) * 128, :], t[i % 2][:, 0:w])
        else:
            npart = srct.shape[0] // 128
            for i in range(npart):
                w = srct.shape[1]
                S.dma("sync", tb[:, 0:w], srct[i * 128:(i + 1) * 128, :])
                S.copy(t[i % 2][:, 0:w], tb[:, 0:w])
                S.dma("sync", ap[i * 128:(i + 1) * 128, :], t[i % 2][:, 0:w])
    S.nopwait("sync", ["out_fm"] + ["dbg_" + n for n in ctx["dbg"]])


def _prep_inputs(inputs):
    f32 = np.float32
    def ct(a):
        return np.ascontiguousarray(a, dtype=f32)
    base = {}
    base["wwT"] = ct(inputs["L0_attn_W_write"].T)
    base["wrT"] = ct(inputs["L0_attn_W_read"].T)
    wba = np.zeros((D, 40), np.float32)
    wba[:, 0:8] = inputs["L0_attn_W_beta"].T
    wba[:, 32:40] = inputs["L0_attn_W_alpha"].T
    base["wba"] = ct(wba)
    base["a_log"] = ct(inputs["L0_attn_A_log"].reshape(1, 8))
    base["dt_bias"] = ct(inputs["L0_attn_dt_bias"].reshape(1, 8))
    for l in range(2):
        base[f"f_wqkv{l}"] = ct(inputs["L0_ffn_wqkv"][l])
        base[f"f_wo{l}"] = ct(inputs["L0_ffn_wo"][l])
        w1 = inputs["L0_ffn_w1"][l]
        b1 = inputs["L0_ffn_b1"][l]
        w1a = np.zeros((D, 2816), f32); w1a[:, :2730] = w1[:, :2730]
        w1g = np.zeros((D, 2816), f32); w1g[:, :2730] = w1[:, 2730:]
        b1a = np.zeros((2816,), f32); b1a[:2730] = b1[:2730]
        b1g = np.zeros((2816,), f32); b1g[:2730] = b1[2730:]
        base[f"f_w1a{l}"] = ct(w1a)
        base[f"f_w1g{l}"] = ct(w1g)
        base[f"f_b1a{l}"] = ct(b1a.reshape(22, 128).T)
        base[f"f_b1g{l}"] = ct(b1g.reshape(22, 128).T)
        w2 = np.zeros((2816, D), f32); w2[:2730] = inputs["L0_ffn_w2"][l]
        base[f"f_w2{l}"] = ct(w2)
        base[f"f_b2{l}"] = ct(inputs["L0_ffn_b2"][l].reshape(8, 128).T)
        base[f"f_lng{l}"] = ct(inputs["L0_ffn_ln_g"][l].reshape(8, 128).T)
        base[f"f_lnb{l}"] = ct(inputs["L0_ffn_ln_b"][l].reshape(8, 128).T)
        base[f"f_qs{l}"] = ct(np.tile(inputs["L0_ffn_q_scale"][l], 2).reshape(128, 1))
        base[f"f_ks{l}"] = ct(np.tile(inputs["L0_ffn_k_scale"][l], 2).reshape(128, 1))
    wqkv = inputs["L1_attn_w_qkv"]
    base["a1_wqT"] = ct(wqkv[0:1024].T)
    base["a1_wkT"] = ct(wqkv[1024:2048].T)
    base["a1_wvT"] = ct(wqkv[2048:3072].T)
    base["a1_woT"] = ct(inputs["L1_attn_w_out"].T)
    base["a1_bo"] = ct(inputs["L1_attn_b_out"].reshape(8, 128).T)
    base["g_wg"] = ct(inputs["L1_ffn_w_gate"])
    base["g_wu"] = ct(inputs["L1_ffn_w_up"])
    base["g_wd"] = ct(inputs["L1_ffn_w_down"])
    for k, v in CONSTS.items():
        base[k] = ct(v)
    maps = []
    for b in range(8):
        m = dict(base)
        m["x_fm"] = ct(inputs["L0_attn_out"][b].T)
        maps.append(m)
    return maps


_NC_CACHE = {}


def kernel(**inputs):
    upto = int(inputs.pop("_upto", 9))
    dbg = inputs.pop("_dbg", ())
    SERIAL[0] = bool(inputs.pop("_serial", 0))
    key = (upto, tuple(dbg), SERIAL[0])
    if key not in _NC_CACHE:
        _NC_CACHE[key] = build_program(upto=upto, dbg=dbg)
    nc = _NC_CACHE[key]
    in_maps = _prep_inputs(inputs)
    res = run_bass_kernel_spmd(nc, in_maps, core_ids=list(range(8)))
    out = np.stack([np.asarray(res.results[b]["out_fm"]).T for b in range(8)])
    if dbg:
        kernel._dbg = [{n: np.asarray(res.results[b]["dbg_" + n]) for n, _ in dbg}
                       for b in range(8)]
    return np.ascontiguousarray(out, dtype=np.float32)
